# version 01
# baseline (speedup 1.0000x reference)
import math
import numpy as np
import ml_dtypes
import concourse.bass as bass
import concourse.mybir as mybir
from concourse.bass_utils import run_bass_kernel_spmd

F32 = mybir.dt.float32
BF16 = mybir.dt.bfloat16
AF = mybir.ActivationFunctionType
ALU = mybir.AluOpType
AX = mybir.AxisListType

NCORE = 8
B, S, D = 2, 8192, 1024
NH, DK = 16, 64
NPAIR = 8
CH = 256
NCHUNK_OWN = 8
TOK = CH * NCHUNK_OWN
DC = 256
FF = 4096
NE = 8
KT = 128
NKT = S // KT
VREC = 130
EPS = 1e-5

ENGINES = ("sync", "tensor", "vector", "scalar", "gpsimd")


class Buf:
    __slots__ = ("w", "r", "name")

    def __init__(self, name=""):
        self.w = None
        self.r = []
        self.name = name


class Prog:
    def __init__(self, nc, sems):
        self.nc = nc
        self.sems = sems
        self.ops = {e: [] for e in ENGINES}
        self.cnt = {e: 0 for e in ENGINES}

    def add(self, eng, emit, reads=(), writes=(), inc=1):
        need = {}

        def _need(tok):
            if tok is None:
                return
            e, v = tok
            if e == eng and e == "tensor":
                return
            if need.get(e, 0) < v:
                need[e] = v

        for b in reads:
            _need(b.w)
        for b in writes:
            _need(b.w)
            for t in b.r:
                _need(t)
        self.cnt[eng] += inc
        tok = (eng, self.cnt[eng])
        self.ops[eng].append((emit, need, inc))
        for b in reads:
            b.r.append(tok)
        for b in writes:
            b.w = tok
            b.r = []
        return tok

    def finalize(self, block):
        for eng in ENGINES:
            ops = self.ops[eng]
            if not ops:
                continue
            getattr(block, eng)(self._mk_body(eng, ops))

    def _mk_body(self, eng, ops):
        sems = self.sems

        def body2(E):
            water = {}
            for emit, need, inc in ops:
                for e2 in sorted(need):
                    v = need[e2]
                    if water.get(e2, 0) < v:
                        E.wait_ge(sems[e2], v)
                        water[e2] = v
                ins = emit(E, sems[eng])
                if ins is not None:
                    ins.then_inc(sems[eng], inc)
            if eng == "sync":
                E.wait_ge(sems[eng], self.cnt[eng])
        return body2


def _bf(x):
    return np.asarray(x).astype(ml_dtypes.bfloat16)


def own_token_idx(j):
    idx = []
    for m in range(NCHUNK_OWN):
        c = 4 * m + j
        idx.extend(range(c * CH, (c + 1) * CH))
    return np.array(idx)


_OWN_IDX = [own_token_idx(j) for j in range(4)]


def shard_fmajor(xb):
    return [np.ascontiguousarray(xb[_OWN_IDX[j], :].T) for j in range(4)]


def unshard_fmajor(shards):
    out = np.empty((S, D), dtype=shards[0].dtype)
    for j in range(4):
        out[_OWN_IDX[j], :] = shards[j].T
    return out


def gather_tokmajor(shards):
    out = np.empty((S, shards[0].shape[1]), dtype=shards[0].dtype)
    for j in range(4):
        out[_OWN_IDX[j], :] = shards[j]
    return out


def build_maskwin(j):
    mw = np.zeros((8, KT, CH), dtype=np.float32)
    for t in range(8):
        off = (t - 2 * j) * KT
        for k in range(KT):
            qlo = off + k
            if qlo < 0:
                mw[t, k, :] = 1.0
            elif qlo < CH:
                mw[t, k, qlo:] = 1.0
    return _bf(mw.reshape(8 * KT, CH))


def rope_tables(j):
    pos = _OWN_IDX[j].astype(np.float64)
    i = np.arange(32, dtype=np.float64)
    freqs = 1.0 / (10000.0 ** (2.0 * i / DK))
    ang = np.outer(pos, freqs)
    cos = np.cos(ang).astype(np.float32)
    sin = np.sin(ang).astype(np.float32)
    return np.tile(cos, (1, 16)), np.tile(sin, (1, 16))


def emit_dma(p, q, out_ap, in_ap, reads, writes, **kw):
    return p.add(q, lambda E, S_, o=out_ap, i=in_ap: E.dma_start(out=o, in_=i, **kw),
                 reads=reads, writes=writes, inc=16)


class Stack:

    def __init__(self):
        self.cms = []

    def enter(self, cm):
        v = cm.__enter__()
        self.cms.append(cm)
        return v

    def close(self):
        for cm in reversed(self.cms):
            cm.__exit__(None, None, None)


def sbuf(st, nc, name, shape, dt):
    return st.enter(nc.sbuf_tensor(name, shape, dt))


def psum(st, nc, name, shape):
    return st.enter(nc.psum_tensor(name, shape, F32))


def build_launch_A():
    nc = bass.Bass("TRN2", target_bir_lowering=False)
    st = Stack()
    xT = nc.dram_tensor("xT", [D, TOK], F32, kind="ExternalInput").ap()
    wdkv = nc.dram_tensor("wdkv", [D, DC], F32, kind="ExternalInput").ap()
    o_ckv = nc.dram_tensor("o_ckv", [DC, TOK], BF16, kind="ExternalOutput").ap()
    o_xn = nc.dram_tensor("o_xn", [D, TOK], BF16, kind="ExternalOutput").ap()
    o_sq = nc.dram_tensor("o_sq", [128, 8], F32, kind="ExternalOutput").ap()

    x = sbuf(st, nc, "x", [128, 8 * TOK], F32)
    xn32 = sbuf(st, nc, "xn32", [128, 8 * TOK], F32)
    xn16 = sbuf(st, nc, "xn16", [128, 8 * TOK], BF16)
    sq = sbuf(st, nc, "sq", [128, 512], F32)
    ones_col = sbuf(st, nc, "ones_col", [128, 1], F32)
    ones_row = sbuf(st, nc, "ones_row", [1, 128], F32)
    cEPS = sbuf(st, nc, "cEPS", [1, 1], F32)
    stat = sbuf(st, nc, "stat", [1, 5 * 512], F32)
    wk16 = sbuf(st, nc, "wk16", [128, 8 * DC], BF16)
    ckv = sbuf(st, nc, "ckv", [128, 2 * TOK], BF16)
    seqp = sbuf(st, nc, "seqp", [128, 8], F32)

    ps_sx = st.enter(nc.psum_tensor("ps_sx", [1, 512], F32))
    ps_sq = st.enter(nc.psum_tensor("ps_sq", [1, 512], F32))
    ps_b = st.enter(nc.psum_tensor("ps_b", [128, 1024], F32))
    ps_c = st.enter(nc.psum_tensor("ps_c", [128, 512], F32))
    ps_c2 = st.enter(nc.psum_tensor("ps_c2", [128, 512], F32))

    sems = {e: st.enter(nc.semaphore("sem_" + e)) for e in ENGINES}
    p = Prog(nc, sems)

    bx = Buf("x")
    bxn32 = Buf("xn32")
    bxn16 = Buf("xn16")
    bsq = Buf("sq")
    bones = Buf("ones")
    bstat = Buf("stat")
    bwk = Buf("wk")
    bckv = Buf("ckv")
    bseqp = Buf("seqp")
    bps_sx = Buf("ps_sx")
    bps_sq = Buf("ps_sq")
    bps_b = Buf("ps_b")
    bps_c = [Buf("ps_c"), Buf("ps_c2")]
    bo_ckv = Buf("o_ckv")
    bo_xn = Buf("o_xn")
    bo_sq = Buf("o_sq")

    emit_dma(p, "sync", x[:, :].rearrange("p (a t) -> p a t", t=TOK),
             xT.rearrange("(a p) t -> p a t", p=128), (), (bx,))
    emit_dma(p, "gpsimd", wk16[:, :].rearrange("p (a c) -> p a c", c=DC),
             wdkv.rearrange("(a p) c -> p a c", p=128), (), (bwk,))
    p.add("vector", lambda E, S_: E.memset(ones_col[:, :], 1.0), (), (bones,))
    p.add("vector", lambda E, S_: E.memset(ones_row[:, :], 1.0), (), (bones,))
    p.add("vector", lambda E, S_: E.memset(cEPS[:, :], EPS), (), (bones,))

    V512 = 512

    def ln_tile(tc):
        t0 = tc * V512
        def mm_sx(E, S_):
            for fc in range(8):
                ins = E.matmul(ps_sx[:, :], ones_col[:, :],
                               x[:, fc * TOK + t0: fc * TOK + t0 + V512],
                               start=(fc == 0), stop=(fc == 7))
            return ins
        p.add("tensor", mm_sx, (bx, bones), (bps_sx,))
        for fc in range(8):
            p.add("scalar",
                  lambda E, S_, fc=fc: E.activation(sq[:, :], x[:, fc * TOK + t0: fc * TOK + t0 + V512], AF.Square),
                  (bx,), (bsq,))
            p.add("tensor",
                  lambda E, S_, fc=fc: E.matmul(ps_sq[:, :], ones_col[:, :], sq[:, :],
                                                start=(fc == 0), stop=(fc == 7)),
                  (bsq, bones), (bps_sq,))
        mu = stat[:, 0:V512]
        m2 = stat[:, V512:2 * V512]
        var = stat[:, 2 * V512:3 * V512]
        rstd = stat[:, 3 * V512:4 * V512]
        mur = stat[:, 4 * V512:5 * V512]
        p.add("vector", lambda E, S_: E.tensor_scalar_mul(mu, ps_sx[:, :], 1.0 / D), (bps_sx,), (bstat,))
        p.add("vector", lambda E, S_: E.tensor_scalar_mul(m2, ps_sq[:, :], 1.0 / D), (bps_sq,), (bstat,))
        p.add("vector", lambda E, S_: E.tensor_mul(var, mu, mu), (bstat,), (bstat,))
        p.add("vector", lambda E, S_: E.tensor_sub(var, m2, var), (bstat,), (bstat,))
        p.add("scalar", lambda E, S_: E.activation(var, var, AF.Ln, bias=cEPS[:, :]), (bstat,), (bstat,))
        p.add("scalar", lambda E, S_: E.activation(rstd, var, AF.Exp, scale=-0.5), (bstat,), (bstat,))
        p.add("vector", lambda E, S_: E.tensor_mul(mur, mu, rstd), (bstat,), (bstat,))
        p.add("tensor", lambda E, S_: E.matmul(ps_b[:, 0:V512], ones_row[:, :], rstd, start=True, stop=True),
              (bstat, bones), (bps_b,))
        p.add("tensor", lambda E, S_: E.matmul(ps_b[:, V512:1024], ones_row[:, :], mur, start=True, stop=True),
              (bstat, bones), (bps_b,))
        for fc in range(8):
            cols = slice(fc * TOK + t0, fc * TOK + t0 + V512)
            p.add("vector",
                  lambda E, S_, cols=cols: E.scalar_tensor_tensor(
                      xn32[:, cols], x[:, cols], 1.0, ps_b[:, V512:1024],
                      op0=ALU.mult, op1=ALU.subtract),
                  (bx, bps_b), (bxn32,))
            p.add("vector",
                  lambda E, S_, cols=cols: E.tensor_tensor(
                      xn32[:, cols], xn32[:, cols], ps_b[:, 0:V512], op=ALU.mult),
                  (bps_b,), (bxn32,))
            p.add("vector",
                  lambda E, S_, cols=cols: E.tensor_copy(xn16[:, cols], xn32[:, cols]),
                  (bxn32,), (bxn16,))

    for tc in range(4):
        ln_tile(tc)

    for fc in range(8):
        p.add("vector",
              lambda E, S_, fc=fc: E.tensor_reduce(
                  seqp[:, fc:fc + 1], xn32[:, fc * TOK:(fc + 1) * TOK], axis=AX.X, op=ALU.add),
              (bxn32,), (bseqp,))

    for cc in range(2):
        for tc in range(4):
            t0 = tc * V512
            pscc = (ps_c, ps_c2)[tc % 2]
            bpscc = bps_c[tc % 2]

            def mm_ckv(E, S_, cc=cc, t0=t0, pscc=pscc):
                for fc in range(8):
                    ins = E.matmul(pscc[:, :],
                                   wk16[:, fc * DC + cc * 128: fc * DC + cc * 128 + 128],
                                   xn16[:, fc * TOK + t0: fc * TOK + t0 + V512],
                                   start=(fc == 0), stop=(fc == 7))
                return ins
            p.add("tensor", mm_ckv, (bwk, bxn16), (bpscc,))
            p.add("vector",
                  lambda E, S_, cc=cc, t0=t0, pscc=pscc: E.tensor_copy(
                      ckv[:, cc * TOK + t0: cc * TOK + t0 + V512], pscc[:, :]),
                  (bpscc,), (bckv,))

    emit_dma(p, "sync", o_ckv.rearrange("(a p) t -> p a t", p=128),
             ckv[:, :].rearrange("p (a t) -> p a t", t=TOK), (bckv,), (bo_ckv,))
    emit_dma(p, "sync", o_xn.rearrange("(a p) t -> p a t", p=128),
             xn16[:, :].rearrange("p (a t) -> p a t", t=TOK), (bxn16,), (bo_xn,))
    emit_dma(p, "sync", o_sq[:, :], seqp[:, :], (bseqp,), (bo_sq,))

    with nc.Block() as block:
        p.finalize(block)
    st.close()
    return nc


def build_maskC():
    def for_j(j):
        m = np.ones((128, 8, 2, CH), dtype=np.float32)
        for t in range(8):
            off = (t - 2 * j) * KT
            for k in range(KT):
                qlo = off + k
                if qlo >= CH:
                    m[k, t, :, :] = 0.0
                elif qlo > 0:
                    m[k, t, :, :qlo] = 0.0
        return _bf(m.reshape(128, 8 * 512))
    return [for_j(j) for j in range(4)]


def emit_pair_norm(p, h, pair, src_ap, src_buf, ncols):
    nchunk = (ncols + 511) // 512
    for c in range(nchunk):
        cols = slice(c * 512, min((c + 1) * 512, ncols))
        n = cols.stop - cols.start
        p.add("scalar",
              lambda E, S_, cols=cols, n=n: E.activation(h.sq[:, 0:n], src_ap[:, cols], AF.Square),
              (src_buf,), (h.bsq,))
        p.add("tensor",
              lambda E, S_, n=n: E.matmul(h.psn[0:1, 0:n], h.ones_col16[:, :], h.sq[:, 0:n],
                                          start=True, stop=True),
              (h.bsq, h.bones), (h.bpsn,))
        p.add("vector",
              lambda E, S_, n=n: E.tensor_reduce(h.tnrm[0:1, 0:1], h.psn[0:1, 0:n],
                                                 axis=AX.X, op=ALU.max),
              (h.bpsn,), (h.btnrm,))
        p.add("vector",
              lambda E, S_, pair=pair: E.tensor_max(h.nrm[0:1, pair:pair + 1],
                                                    h.nrm[0:1, pair:pair + 1], h.tnrm[0:1, 0:1]),
              (h.btnrm,), (h.bnrm,))


def emit_bias_from_norms(p, h, pair, qn_ap, kn_ap, bq, bk):
    p.add("vector", lambda E, S_: E.tensor_mul(h.tnrm[0:1, 0:1], qn_ap, kn_ap),
          (bq, bk), (h.btnrm,))
    p.add("scalar", lambda E, S_: E.activation(h.tnrm[0:1, 1:2], h.tnrm[0:1, 0:1], AF.Sqrt),
          (h.btnrm,), (h.btnrm,))
    p.add("scalar", lambda E, S_: E.activation(h.tnrm[0:1, 0:1], h.tnrm[0:1, 1:2],
                                               AF.Copy, scale=-0.13125),
          (h.btnrm,), (h.btnrm,))
    p.add("tensor",
          lambda E, S_: E.matmul(h.psw[:, 0:1], h.ones_row[:, :], h.tnrm[0:1, 0:1],
                                 start=True, stop=True),
          (h.btnrm, h.bones), (h.bpsw,))
    p.add("vector",
          lambda E, S_, pair=pair: E.tensor_copy(h.Bcol[:, pair:pair + 1], h.psw[:, 0:1]),
          (h.bpsw,), (h.bBcol,))


_ATT_NPAIR = NPAIR
_ATT_NSECT = NCHUNK_OWN


def emit_attention_pairs(p, h, Kscr, Vscr, bKscr, bVscr, prelude):
    psS = [(h.psS0, h.bpsS0), (h.psS1, h.bpsS1)]
    Pb = [(h.P0, h.bP0), (h.P1, h.bP1)]
    gctr = 0
    for pair in range(_ATT_NPAIR):
        emit_dma(p, "sync", h.kfull, Kscr[pair, :, :], (bKscr,), (h.bkc0, h.bkc1))
        emit_dma(p, "scalar", h.vfull, Vscr[pair, :, :], (bVscr,), (h.bvc0, h.bvc1))
        qt, bqt = prelude(pair)
        for m in range(_ATT_NSECT):
            nkt = 8 * m + 8
            qcols = slice(m * CH, (m + 1) * CH)
            for g in range(nkt // 2):
                i0 = 2 * g
                ps, bps = psS[gctr % 2]
                pb, bpb = Pb[gctr % 2]
                gctr += 1

                def mm_scores(E, S_, i0=i0, ps=ps, qt=qt, qcols=qcols):
                    for hh in range(2):
                        for l in range(2):
                            i = i0 + l
                            ins = E.matmul(
                                ps[:, hh * 512 + l * 256: hh * 512 + l * 256 + 256],
                                h.kfull[hh * 64:(hh + 1) * 64, i * KT:(i + 1) * KT],
                                qt[hh * 64:(hh + 1) * 64, qcols],
                                start=True, stop=True)
                    return ins
                p.add("tensor", mm_scores, (h.bkc0, h.bkc1, bqt), (bps,))
                p.add("scalar",
                      lambda E, S_, pb=pb, ps=ps, pair=pair: E.activation(
                          pb[:, 0:1024], ps[:, 0:1024], AF.Exp,
                          scale=0.125, bias=h.Bcol[:, pair:pair + 1]),
                      (bps, h.bBcol), (bpb,))
                for l in range(2):
                    i = i0 + l
                    if i >= 8 * m:
                        t = i - 8 * m
                        pv3 = pb[:, 0:1024].rearrange(
                            "p (hh l c) -> p hh l c", hh=2, l=2)[:, :, l:l + 1, :]
                        mk3 = h.mskC[:, t * 512:(t + 1) * 512].rearrange(
                            "p (hh o c) -> p hh o c", hh=2, o=1)
                        p.add("vector",
                              lambda E, S_, pv3=pv3, mk3=mk3: E.tensor_tensor(
                                  pv3, pv3, mk3, op=ALU.mult),
                              (h.bmskC,), (bpb,))

                def mm_pv(E, S_, i0=i0, pb=pb, nkt=nkt):
                    for hh in range(2):
                        for l in range(2):
                            i = i0 + l
                            ins = E.matmul(
                                h.psO[0:65, hh * 256: hh * 256 + 256],
                                h.vfull[:, i * VREC + hh * 65: i * VREC + hh * 65 + 65],
                                pb[:, hh * 512 + l * 256: hh * 512 + l * 256 + 256],
                                start=(i == 0 and hh == 0),
                                stop=(i == nkt - 1 and hh == 1))
                    return ins
                p.add("tensor", mm_pv, (h.bvc0, h.bvc1, bpb), (h.bpsO,))

            p.add("vector", lambda E, S_: E.reciprocal(h.rs[:, :], h.psO[64:65, 0:512]),
                  (h.bpsO,), (h.brs,))
            p.add("tensor",
                  lambda E, S_: E.matmul(h.psw[0:64, 0:512], h.ones_row64[:, :], h.rs[:, :],
                                         start=True, stop=True),
                  (h.brs, h.bones), (h.bpsw,))
            p.add("vector", lambda E, S_: E.tensor_copy(h.rbc[:, :], h.psw[0:64, 0:512]),
                  (h.bpsw,), (h.brbc,))
            osi = (pair * NCHUNK_OWN + m) % 2
            ost, bost = ((h.OTst0, h.bOTst0), (h.OTst1, h.bOTst1))[osi]
            for hh in range(2):
                p.add("vector",
                      lambda E, S_, hh=hh, ost=ost: E.scalar_tensor_tensor(
                          ost[0:64, hh * 256: hh * 256 + 256],
                          h.psO[0:64, hh * 256: hh * 256 + 256], 1.0,
                          h.rbc[:, hh * 256: hh * 256 + 256],
                          op0=ALU.mult, op1=ALU.mult),
                      (h.bpsO, h.brbc), (bost,))
            emit_dma(p, "sync",
                     h.OTd[2 * pair:2 * pair + 2, :, qcols.start:qcols.stop].rearrange(
                         "hh d q -> d hh q"),
                     ost[0:64, :].rearrange("d (hh q) -> d hh q", hh=2),
                     (bost,), (h.bOTd,))


class NS:
    pass


def emit_ln(p, e, x_ap, bx, out16_ap, bout16):
    V = 512
    for tc in range(4):
        t0 = tc * V

        def mm_sx(E, S_, t0=t0):
            for fc in range(8):
                ins = E.matmul(e.ps_sx, e.ones_col[:, :],
                               x_ap[:, fc * TOK + t0: fc * TOK + t0 + V],
                               start=(fc == 0), stop=(fc == 7))
            return ins
        p.add("tensor", mm_sx, (bx, e.bones), (e.bps_sx,))
        for fc in range(8):
            p.add("scalar",
                  lambda E, S_, fc=fc, t0=t0: E.activation(
                      e.sq32[:, :], x_ap[:, fc * TOK + t0: fc * TOK + t0 + V], AF.Square),
                  (bx,), (e.bsq32,))
            p.add("tensor",
                  lambda E, S_, fc=fc: E.matmul(e.ps_sq, e.ones_col[:, :], e.sq32[:, :],
                                                start=(fc == 0), stop=(fc == 7)),
                  (e.bsq32, e.bones), (e.bps_sq,))
        mu = e.stat[:, 0:V]
        var = e.stat[:, V:2 * V]
        rstd = e.stat[:, 2 * V:3 * V]
        mur = e.stat[:, 3 * V:4 * V]
        p.add("vector", lambda E, S_: E.tensor_scalar_mul(mu, e.ps_sx, 1.0 / D),
              (e.bps_sx,), (e.bstat,))
        p.add("vector", lambda E, S_: E.tensor_scalar_mul(var, e.ps_sq, 1.0 / D),
              (e.bps_sq,), (e.bstat,))
        p.add("vector", lambda E, S_: E.tensor_mul(rstd, mu, mu), (e.bstat,), (e.bstat,))
        p.add("vector", lambda E, S_: E.tensor_sub(var, var, rstd), (e.bstat,), (e.bstat,))
        p.add("scalar", lambda E, S_: E.activation(var, var, AF.Ln, bias=e.cEPS[:, :]),
              (e.bstat, e.bones), (e.bstat,))
        p.add("scalar", lambda E, S_: E.activation(rstd, var, AF.Exp, scale=-0.5),
              (e.bstat,), (e.bstat,))
        p.add("vector", lambda E, S_: E.tensor_mul(mur, mu, rstd), (e.bstat,), (e.bstat,))
        p.add("tensor",
              lambda E, S_: E.matmul(e.ps_r, e.ones_row[:, :], rstd, start=True, stop=True),
              (e.bstat, e.bones), (e.bps_r,))
        p.add("tensor",
              lambda E, S_: E.matmul(e.ps_m, e.ones_row[:, :], mur, start=True, stop=True),
              (e.bstat, e.bones), (e.bps_m,))
        for fc in range(8):
            cols = slice(fc * TOK + t0, fc * TOK + t0 + V)
            p.add("vector",
                  lambda E, S_, cols=cols: E.scalar_tensor_tensor(
                      e.sq32[:, :], x_ap[:, cols], 1.0, e.ps_m,
                      op0=ALU.mult, op1=ALU.subtract),
                  (bx, e.bps_m), (e.bsq32,))
            p.add("vector",
                  lambda E, S_, cols=cols: E.tensor_tensor(
                      out16_ap[:, cols], e.sq32[:, :], e.ps_r, op=ALU.mult),
                  (e.bsq32, e.bps_r), (bout16,))


_B_EN = {"router", "kvprep", "cq", "attn", "wo", "moe", "ln1", "l1prep"}


def build_launch_B():
    nc = bass.Bass("TRN2", target_bir_lowering=False)
    st = Stack()
    xT = nc.dram_tensor("xT", [D, TOK], F32, kind="ExternalInput").ap()
    xn0i = nc.dram_tensor("xn0i", [D, TOK], BF16, kind="ExternalInput").ap()
    ckvF = nc.dram_tensor("ckvF", [DC, S], BF16, kind="ExternalInput").ap()
    seqps = nc.dram_tensor("seqps", [4, 128, 8], F32, kind="ExternalInput").ap()
    mskCi = nc.dram_tensor("mskCi", [128, 8 * 512], BF16, kind="ExternalInput").ap()
    id128i = nc.dram_tensor("id128i", [128, 128], BF16, kind="ExternalInput").ap()
    w_dq = nc.dram_tensor("w_dq", [D, DC], F32, kind="ExternalInput").ap()
    w_uq = nc.dram_tensor("w_uq", [DC, D], F32, kind="ExternalInput").ap()
    w_uk = nc.dram_tensor("w_uk", [DC, D], F32, kind="ExternalInput").ap()
    w_uv = nc.dram_tensor("w_uv", [DC, D], F32, kind="ExternalInput").ap()
    w_o = nc.dram_tensor("w_o", [D, D], F32, kind="ExternalInput").ap()
    gate_w = nc.dram_tensor("gate_w", [D, NE], F32, kind="ExternalInput").ap()
    w1 = nc.dram_tensor("w1", [NE, D, FF], F32, kind="ExternalInput").ap()
    b1T = nc.dram_tensor("b1T", [NE, 128, 32], F32, kind="ExternalInput").ap()
    w2 = nc.dram_tensor("w2", [NE, FF, D], F32, kind="ExternalInput").ap()
    b2 = nc.dram_tensor("b2", [NE, D], F32, kind="ExternalInput").ap()
    wqT = nc.dram_tensor("wqT", [D, D], F32, kind="ExternalInput").ap()
    wkT = nc.dram_tensor("wkT", [D, D], F32, kind="ExternalInput").ap()
    wvT = nc.dram_tensor("wvT", [D, D], F32, kind="ExternalInput").ap()
    cosT = nc.dram_tensor("cosT", [TOK, 512], F32, kind="ExternalInput").ap()
    sinT = nc.dram_tensor("sinT", [TOK, 512], F32, kind="ExternalInput").ap()

    o_x1 = nc.dram_tensor("o_x1", [D, TOK], F32, kind="ExternalOutput").ap()
    o_xn1 = nc.dram_tensor("o_xn1", [D, TOK], BF16, kind="ExternalOutput").ap()
    o_k1 = nc.dram_tensor("o_k1", [TOK, D], BF16, kind="ExternalOutput").ap()
    o_v1 = nc.dram_tensor("o_v1", [TOK, D], BF16, kind="ExternalOutput").ap()
    o_q1 = nc.dram_tensor("o_q1", [TOK, D], BF16, kind="ExternalOutput").ap()

    Kscr = nc.dram_tensor("Kscr", [NPAIR, 128, NKT * KT], BF16, kind="Internal").ap()
    Vscr = nc.dram_tensor("Vscr", [NPAIR, 128, NKT * VREC], BF16, kind="Internal").ap()
    OTd = nc.dram_tensor("OTd", [NH, 64, TOK], BF16, kind="Internal").ap()

    x = sbuf(st, nc, "x", [128, 8 * TOK], F32)
    xn16 = sbuf(st, nc, "xn16", [128, 8 * TOK], BF16)
    OTst0 = sbuf(st, nc, "OTst0", [64, 512], BF16)
    OTst1 = sbuf(st, nc, "OTst1", [64, 512], BF16)
    OTtc = sbuf(st, nc, "OTtc", [64, 16 * 128], BF16)
    pool1 = sbuf(st, nc, "pool1", [128, 16512], BF16)
    kc0 = pool1[:, 0:4096]
    kc1 = pool1[:, 4096:8192]
    vc0 = pool1[:, 8192:12352]
    vc1 = pool1[:, 12352:16512]
    P0 = sbuf(st, nc, "P0", [128, 1536], BF16)
    P1 = sbuf(st, nc, "P1", [128, 1536], BF16)
    cq = sbuf(st, nc, "cq", [128, 2 * TOK], BF16)
    QT0 = sbuf(st, nc, "QT0", [128, TOK], BF16)
    mskC = sbuf(st, nc, "mskC", [128, 8 * 512], BF16)
    id128 = sbuf(st, nc, "id128", [128, 128], BF16)
    Bcol = sbuf(st, nc, "Bcol", [128, 8], F32)
    nrm = sbuf(st, nc, "nrm", [1, 16], F32)
    tnrm = sbuf(st, nc, "tnrm", [1, 2], F32)
    sq = sbuf(st, nc, "sqW", [128, 512], BF16)
    sq32 = sbuf(st, nc, "sq32", [128, 512], F32)
    rs = sbuf(st, nc, "rs", [1, 512], F32)
    rbc = sbuf(st, nc, "rbc", [64, 512], F32)
    ones_col = sbuf(st, nc, "ones_col", [128, 1], F32)
    ones_col16 = sbuf(st, nc, "ones_col16", [128, 1], BF16)
    ones_row = sbuf(st, nc, "ones_row", [1, 128], F32)
    ones_row64 = sbuf(st, nc, "ones_row64", [1, 64], F32)
    cEPS = sbuf(st, nc, "cEPS", [1, 1], F32)
    stat = sbuf(st, nc, "stat", [1, 4 * 512], F32)
    seqv = sbuf(st, nc, "seqv", [128, 8], F32)
    seqt = sbuf(st, nc, "seqt", [128, 8], F32)
    gsb = sbuf(st, nc, "gsb", [128, 8 * 8], F32)
    comb = sbuf(st, nc, "comb", [1, 32], F32)
    comb8 = sbuf(st, nc, "comb8", [8, 1], BF16)
    combB = sbuf(st, nc, "combB", [128, 8], F32)
    b2sb = sbuf(st, nc, "b2sb", [8, D], BF16)
    b1sb = sbuf(st, nc, "b1sb", [128, 32], F32)
    ckvc0 = sbuf(st, nc, "ckvc0", [128, 1024], BF16)
    ckvc1 = ckvc0
    kst0 = sbuf(st, nc, "kst0", [128, 512], BF16)
    vst0 = sbuf(st, nc, "vst0", [128, 8 * VREC], BF16)
    vst1 = vst0
    ct0 = sbuf(st, nc, "ct0", [128, 512], F32)
    st0_ = sbuf(st, nc, "st0", [128, 512], F32)
    kvsc = sbuf(st, nc, "kvsc", [128, 512], F32)
    kv16 = sbuf(st, nc, "kv16", [128, 1024], BF16)
    wo64 = sbuf(st, nc, "wo64", [64, 16 * 128], BF16)

    psS0 = psum(st, nc, "psS0", [128, 1024])
    psS1 = psum(st, nc, "psS1", [128, 1024])
    psY = psum(st, nc, "psY", [128, 1024])
    psO = psum(st, nc, "psO", [128, 512])
    psw = psum(st, nc, "psw", [128, 512])

    sems = {e: st.enter(nc.semaphore("sem_" + e)) for e in ENGINES}
    p = Prog(nc, sems)

    h = NS()
    h.kc0, h.kc1, h.vc0, h.vc1 = kc0, kc1, vc0, vc1
    h.kfull = pool1[:, 0:8192]
    h.vfull = pool1[:, 8192:16512]
    h.psS0, h.psS1, h.psO, h.psw = psS0, psS1, psO, psw
    h.psY = psY
    h.P0, h.P1, h.mskC, h.id128, h.Bcol = P0, P1, mskC, id128, Bcol
    h.rs, h.rbc = rs, rbc
    h.OTst0, h.OTst1, h.OTd = OTst0, OTst1, OTd
    h.ones_row64, h.ones_row = ones_row64, ones_row
    h.ones_col, h.ones_col16 = ones_col, ones_col16
    h.nrm, h.tnrm, h.sq = nrm, tnrm, sq
    h.psn = psw[0:1, 0:512]
    for nm in ("bkc0", "bkc1", "bvc0", "bvc1", "bpsS0", "bpsS1", "bpsO", "bpsw",
               "bP0", "bP1", "bmskC", "bid", "bBcol", "brs", "brbc",
               "bOTst0", "bOTst1", "bOTd", "bones", "bnrm", "btnrm", "bsq",
               "bpsY"):
        setattr(h, nm, Buf(nm))
    h.bpsn = h.bpsw

    bx = Buf("x"); bxn16 = Buf("xn16"); bcq = Buf("cq")
    bQT = [Buf("QT0")]
    QT = [QT0]
    bOTtc = Buf("OTtc")
    bKscr = Buf("Kscr"); bVscr = Buf("Vscr")
    bseqv = Buf("seqv"); bseqt = Buf("seqt"); bgsb = Buf("gsb")
    bcomb = Buf("comb"); bcomb8 = Buf("comb8"); bcombB = Buf("combB")
    bb2 = Buf("b2sb"); bb1 = Buf("b1sb")
    _bck = Buf("ckvc0")
    bckvc = [_bck, _bck]
    bkst = [Buf("kst0")]
    _bvs = Buf("vst0")
    bvst = [_bvs, _bvs]
    bct = Buf("ct0"); bst_ = Buf("st0")
    bkvsc = Buf("kvsc"); bkv16 = Buf("kv16")
    bstat = Buf("stat"); bsq32 = Buf("sq32"); bwo = Buf("wo64")
    bo = {k: Buf(k) for k in ("x1", "xn1", "k1", "v1", "q1")}

    env = NS()
    env.ones_col, env.ones_row, env.cEPS = ones_col, ones_row, cEPS
    env.stat, env.bstat, env.sq32, env.bsq32 = stat, bstat, sq32, bsq32
    env.bones = h.bones
    env.ps_sx = psS0[0:1, 0:512]; env.bps_sx = h.bpsS0
    env.ps_sq = psS0[0:1, 512:1024]; env.bps_sq = h.bpsS0
    env.ps_r = psw[:, 0:512]; env.bps_r = h.bpsw
    env.ps_m = psO[:, 0:512]; env.bps_m = h.bpsO

    emit_dma(p, "sync", x[:, :].rearrange("p (a t) -> p a t", t=TOK),
             xT.rearrange("(a p) t -> p a t", p=128), (), (bx,))
    emit_dma(p, "sync", xn16[:, :].rearrange("p (a t) -> p a t", t=TOK),
             xn0i.rearrange("(a p) t -> p a t", p=128), (), (bxn16,))
    emit_dma(p, "sync", mskC[:, :], mskCi, (), (h.bmskC,))
    emit_dma(p, "sync", id128[:, :], id128i, (), (h.bid,))
    p.add("vector", lambda E, S_: E.memset(ones_col[:, :], 1.0), (), (h.bones,))
    p.add("vector", lambda E, S_: E.memset(ones_col16[:, :], 1.0), (), (h.bones,))
    p.add("vector", lambda E, S_: E.memset(ones_row[:, :], 1.0), (), (h.bones,))
    p.add("vector", lambda E, S_: E.memset(ones_row64[:, :], 1.0), (), (h.bones,))
    p.add("vector", lambda E, S_: E.memset(cEPS[:, :], EPS), (), (h.bones,))
    p.add("vector", lambda E, S_: E.memset(nrm[:, :], 0.0), (), (h.bnrm,))

    if "router" not in _B_EN:
        p.add("vector", lambda E, S_: E.memset(combB[:, :], 0.125), (), (bcombB,))
        p.add("vector", lambda E, S_: E.memset(comb8[:, :], 0.125), (), (bcomb8,))
    if "router" in _B_EN:
        emit_dma(p, "sync", seqv[:, :], seqps[0, :, :], (), (bseqv,))
        for r in range(1, 4):
            emit_dma(p, "sync", seqt[:, :], seqps[r, :, :], (), (bseqt,))
            p.add("vector", lambda E, S_: E.tensor_add(seqv[:, :], seqv[:, :], seqt[:, :]),
                  (bseqt,), (bseqv,))
        emit_dma(p, "sync", gsb[:, :].rearrange("p (a e) -> p a e", e=NE),
                 gate_w.rearrange("(a p) e -> p a e", p=128), (), (bgsb,))

        def mm_logits(E, S_):
            for fc in range(8):
                ins = E.matmul(psw[0:1, 0:8], seqv[:, fc:fc + 1],
                               gsb[:, fc * 8:(fc + 1) * 8], start=(fc == 0), stop=(fc == 7))
            return ins
        p.add("tensor", mm_logits, (bseqv, bgsb), (h.bpsw,))
        lg = comb[0:1, 0:8]; pr = comb[0:1, 8:16]
        t1 = comb[0:1, 16:17]; t2 = comb[0:1, 17:18]; t3 = comb[0:1, 18:19]
        msk = comb[0:1, 19:20 + 7]
        msk = comb[0:1, 19:27]
        vv = comb[0:1, 27:28]
        cw = comb[0:1, 28:29]
        p.add("vector", lambda E, S_: E.tensor_scalar_mul(lg, psw[0:1, 0:8], 1.0 / S),
              (h.bpsw,), (bcomb,))
        p.add("vector", lambda E, S_: E.tensor_reduce(t1, lg, axis=AX.X, op=ALU.max),
              (bcomb,), (bcomb,))
        p.add("vector", lambda E, S_: E.tensor_scalar(pr, lg, t1, None, op0=ALU.subtract),
              (bcomb,), (bcomb,))
        p.add("scalar", lambda E, S_: E.activation(pr, pr, AF.Exp), (bcomb,), (bcomb,))
        p.add("vector", lambda E, S_: E.tensor_reduce(t1, pr, axis=AX.X, op=ALU.add),
              (bcomb,), (bcomb,))
        p.add("vector", lambda E, S_: E.reciprocal(t2, t1), (bcomb,), (bcomb,))
        p.add("vector", lambda E, S_: E.tensor_scalar(pr, pr, t2, None, op0=ALU.mult),
              (bcomb,), (bcomb,))
        p.add("vector", lambda E, S_: E.tensor_reduce(t1, pr, axis=AX.X, op=ALU.max),
              (bcomb,), (bcomb,))
        p.add("vector", lambda E, S_: E.tensor_scalar(msk, pr, t1, None, op0=ALU.is_lt),
              (bcomb,), (bcomb,))
        p.add("vector", lambda E, S_: E.tensor_mul(stat[0:1, 0:8], pr, msk), (bcomb,), (bstat,))
        p.add("vector", lambda E, S_: E.tensor_reduce(t2, stat[0:1, 0:8], axis=AX.X, op=ALU.max),
              (bstat,), (bcomb,))
        p.add("vector", lambda E, S_: E.tensor_scalar(msk, pr, t2, None, op0=ALU.is_ge),
              (bcomb,), (bcomb,))
        p.add("vector", lambda E, S_: E.tensor_add(t3, t1, t2), (bcomb,), (bcomb,))
        p.add("vector", lambda E, S_: E.reciprocal(vv, t3), (bcomb,), (bcomb,))
        p.add("vector", lambda E, S_: E.tensor_mul(stat[0:1, 0:8], pr, msk), (bcomb,), (bstat,))
        p.add("vector", lambda E, S_: E.tensor_scalar(comb[0:1, 0:8], stat[0:1, 0:8], vv, None,
                                                      op0=ALU.mult),
              (bstat,), (bcomb,))
        p.add("vector", lambda E, S_: E.tensor_copy(sq[0:1, 0:8], comb[0:1, 0:8]),
              (bcomb,), (h.bsq,))
        p.add("tensor",
              lambda E, S_: E.matmul(psO[0:8, 0:1], sq[0:1, 0:8], ones_col16[0:1, 0:1],
                                     start=True, stop=True),
              (h.bsq, h.bones), (h.bpsO,))
        p.add("vector", lambda E, S_: E.tensor_copy(comb8[:, :], psO[0:8, 0:1]),
              (h.bpsO,), (bcomb8,))
        p.add("tensor",
              lambda E, S_: E.matmul(psO[0:128, 0:8], ones_row[:, :], comb[0:1, 0:8],
                                     start=True, stop=True),
              (bcomb, h.bones), (h.bpsO,))
        p.add("vector", lambda E, S_: E.tensor_copy(combB[:, :], psO[0:128, 0:8]),
              (h.bpsO,), (bcombB,))

    if "kvprep" in _B_EN:
        wuk16 = pool1[:, 0:2048]
        wuv16 = pool1[:, 4096:6144]
        emit_dma(p, "gpsimd", wuk16.rearrange("p (a o) -> p a o", o=D),
                 w_uk.rearrange("(a p) o -> p a o", p=128), (), (h.bkc0,))
        emit_dma(p, "gpsimd", wuv16.rearrange("p (a o) -> p a o", o=D),
                 w_uv.rearrange("(a p) o -> p a o", p=128), (), (h.bkc1,))
        for pair in range(NPAIR):
            for kc in range(16):
                cols = slice(kc * 512, (kc + 1) * 512)
                ci = (pair * 16 + kc) % 2
                ckvc, bck = (ckvc0, ckvc1)[ci], bckvc[ci]
                emit_dma(p, "sync", ckvc[:, :].rearrange("p (a t) -> p a t", t=512),
                         ckvF.rearrange("(a p) t -> p a t", p=128)[:, :, cols],
                         (), (bck,))

                def mm_k(E, S_, pair=pair, ckvc=ckvc):
                    for cc in range(2):
                        ins = E.matmul(psO[:, 0:512],
                                       wuk16[:, cc * D + pair * 128: cc * D + pair * 128 + 128],
                                       ckvc[:, cc * 512:(cc + 1) * 512],
                                       start=(cc == 0), stop=(cc == 1))
                    return ins
                p.add("tensor", mm_k, (h.bkc0, bck), (h.bpsO,))
                kst, bks = kst0, bkst[0]
                p.add("vector", lambda E, S_, kst=kst: E.tensor_copy(kst[:, :], psO[:, 0:512]),
                      (h.bpsO,), (bks,))
                emit_dma(p, "sync", Kscr[pair, :, cols.start:cols.stop], kst[:, :],
                         (bks,), (bKscr,))
                p.add("scalar", lambda E, S_, kst=kst: E.activation(sq[:, 0:512], kst[:, :], AF.Square),
                      (bks,), (h.bsq,))
                p.add("tensor",
                      lambda E, S_: E.matmul(h.psn, ones_col16[:, :], sq[:, 0:512],
                                             start=True, stop=True),
                      (h.bsq, h.bones), (h.bpsw,))
                p.add("vector",
                      lambda E, S_: E.tensor_reduce(tnrm[0:1, 0:1], h.psn, axis=AX.X, op=ALU.max),
                      (h.bpsw,), (h.btnrm,))
                p.add("vector",
                      lambda E, S_, pair=pair: E.tensor_max(nrm[0:1, pair:pair + 1],
                                                            nrm[0:1, pair:pair + 1],
                                                            tnrm[0:1, 0:1]),
                      (h.btnrm,), (h.bnrm,))

        p.add("vector", lambda E, S_: E.memset(vst0[:, :], 1.0), (), (bvst[0],))
        for kt in range(NKT):
            kcols = slice(kt * KT, (kt + 1) * KT)
            ci = kt % 2
            ckvc, bck = (ckvc0, ckvc1)[ci], bckvc[ci]
            emit_dma(p, "sync", ckvc[:, 0:256].rearrange("p (a t) -> p a t", t=KT),
                     ckvF.rearrange("(a p) t -> p a t", p=128)[:, :, kcols],
                     (), (bck,))
            vst, bvs = (vst0, vst1)[ci], bvst[ci]
            for half in range(2):
                ps_ = (psO, psw)[half]
                bps_ = (h.bpsO, h.bpsw)[half]

                def mm_v(E, S_, half=half, ps_=ps_, ckvc=ckvc):
                    for cc in range(2):
                        ins = E.matmul(ps_[:, 0:512],
                                       ckvc[:, cc * 128:(cc + 1) * 128],
                                       wuv16[:, cc * D + half * 512: cc * D + half * 512 + 512],
                                       start=(cc == 0), stop=(cc == 1))
                    return ins
                p.add("tensor", mm_v, (h.bkc1, bck), (bps_,))
                p.add("vector",
                      lambda E, S_, half=half, ps_=ps_, vst=vst: E.tensor_copy(
                          vst[:, :].rearrange("p (pr s d) -> p pr s d", pr=8, s=2)[
                              :, 4 * half:4 * half + 4, :, 0:64],
                          ps_[:, 0:512].rearrange("p (pr s d) -> p pr s d", pr=4, s=2)),
                      (bps_,), (bvs,))
            emit_dma(p, "sync",
                     Vscr[:, :, kt * VREC:(kt + 1) * VREC].rearrange("r p c -> p r c"),
                     vst[:, :].rearrange("p (r c) -> p r c", c=VREC),
                     (bvs,), (bVscr,))

    if "cq" in _B_EN:
        wdq16 = pool1[:, 2048:4096]
        wuq16 = sbuf(st, nc, "wuq16", [128, 2048], BF16)
        bwuq16 = Buf("wuq16")
        emit_dma(p, "gpsimd", wdq16.rearrange("p (a c) -> p a c", c=DC),
                 w_dq.rearrange("(a p) c -> p a c", p=128), (), (h.bkc0,))
        emit_dma(p, "gpsimd", wuq16[:, :].rearrange("p (a o) -> p a o", o=D),
                 w_uq.rearrange("(a p) o -> p a o", p=128), (), (bwuq16,))
        for cc in range(2):
            for tc in range(4):
                cols = slice(tc * 512, (tc + 1) * 512)

                def mm_cq(E, S_, cc=cc, cols=cols):
                    for fc in range(8):
                        ins = E.matmul(psO[:, 0:512],
                                       wdq16[:, fc * DC + cc * 128: fc * DC + cc * 128 + 128],
                                       xn16[:, fc * TOK + cols.start: fc * TOK + cols.stop],
                                       start=(fc == 0), stop=(fc == 7))
                    return ins
                p.add("tensor", mm_cq, (h.bkc0, bxn16), (h.bpsO,))
                p.add("vector",
                      lambda E, S_, cc=cc, cols=cols: E.tensor_copy(
                          cq[:, cc * TOK + cols.start: cc * TOK + cols.stop], psO[:, 0:512]),
                      (h.bpsO,), (bcq,))

    qpctr = [0]

    def qt_provider_L0(pair):
        qt, bqt = QT[0], bQT[0]
        for tc in range(4):
            cols = slice(tc * 512, (tc + 1) * 512)

            def mm_q(E, S_, pair=pair, cols=cols):
                for cc in range(2):
                    ins = E.matmul(psw[:, 0:512],
                                   wuq16[:, cc * 1024 + pair * 128: cc * 1024 + pair * 128 + 128],
                                   cq[:, cc * TOK + cols.start: cc * TOK + cols.stop],
                                   start=(cc == 0), stop=(cc == 1))
                return ins
            p.add("tensor", mm_q, (bwuq16, bcq), (h.bpsw,))
            p.add("vector",
                  lambda E, S_, qt=qt, cols=cols: E.tensor_copy(qt[:, cols], psw[:, 0:512]),
                  (h.bpsw,), (bqt,))
        emit_pair_norm(p, h, 8 + pair, qt, bqt, TOK)
        emit_bias_from_norms(p, h, pair, nrm[0:1, 8 + pair:9 + pair],
                             nrm[0:1, pair:pair + 1], h.bnrm, h.bnrm)
        return qt, bqt

    if "attn" in _B_EN:
        emit_attention_pairs(p, h, Kscr, Vscr, bKscr, bVscr, qt_provider_L0)

    if "wo" in _B_EN:
        for tc in range(16):
            cols = slice(tc * 128, (tc + 1) * 128)
            emit_dma(p, "sync", OTtc[0:64, :].rearrange("d (hh q) -> d hh q", q=128),
                     OTd[:, :, cols].rearrange("hh d q -> d hh q"),
                     (h.bOTd,), (bOTtc,))
            for oc in range(8):
                emit_dma(p, "gpsimd", wo64[:, :].rearrange("p (hh o) -> p hh o", o=128),
                         w_o.rearrange("(hh d) o -> d hh o", d=64)[:, :, oc * 128:(oc + 1) * 128],
                         (), (bwo,))

                def mm_wo(E, S_):
                    for hh in range(16):
                        ins = E.matmul(psO[:, 0:128],
                                       wo64[:, hh * 128:(hh + 1) * 128],
                                       OTtc[0:64, hh * 128:(hh + 1) * 128],
                                       start=(hh == 0), stop=(hh == 15))
                    return ins
                p.add("tensor", mm_wo, (bwo, bOTtc), (h.bpsO,))
                p.add("vector",
                      lambda E, S_, oc=oc, cols=cols: E.scalar_tensor_tensor(
                          x[:, oc * TOK + cols.start: oc * TOK + cols.stop],
                          psO[:, 0:128], 1.0,
                          x[:, oc * TOK + cols.start: oc * TOK + cols.stop],
                          op0=ALU.mult, op1=ALU.add),
                      (h.bpsO,), (bx,))

    if "moe" in _B_EN:
        emit_dma(p, "gpsimd", b2sb[:, :], b2, (), (bb2,))
        w1q = pool1[:, 0:8192]
        w2q = pool1[:, 8192:16384]
        bh16 = bcq
        h16 = cq
        for e in range(NE):
            emit_dma(p, "sync", b1sb[:, :], b1T[e, :, :], (), (bb1,))
            for fq in range(4):
                emit_dma(p, "gpsimd", w1q.rearrange("p (a c) -> p a c", c=1024),
                         w1[e, :, fq * 1024:(fq + 1) * 1024].rearrange("(a p) c -> p a c", p=128),
                         (), (h.bkc0, h.bkc1))
                emit_dma(p, "gpsimd", w2q.rearrange("p (g c) -> p g c", c=1024),
                         w2[e, fq * 1024:(fq + 1) * 1024, :].rearrange("(g p) c -> p g c", p=128),
                         (), (h.bvc0, h.bvc1))
                for t16 in range(4):
                    tcols = slice(t16 * 512, (t16 + 1) * 512)
                    for fh in range(8):
                        ps_, bps_ = ((psO, h.bpsO), (psw, h.bpsw))[fh % 2]

                        def mm_h(E, S_, fh=fh, tcols=tcols, ps_=ps_):
                            for fc in range(8):
                                ins = E.matmul(
                                    ps_[:, 0:512],
                                    w1q[:, fc * 1024 + fh * 128: fc * 1024 + fh * 128 + 128],
                                    xn16[:, fc * TOK + tcols.start: fc * TOK + tcols.stop],
                                    start=(fc == 0), stop=(fc == 7))
                            return ins
                        p.add("tensor", mm_h, (h.bkc0, h.bkc1, bxn16), (bps_,))
                        p.add("scalar",
                              lambda E, S_, fq=fq, fh=fh, ps_=ps_: E.activation(
                                  h16[:, fh * 512:(fh + 1) * 512], ps_[:, 0:512], AF.Gelu,
                                  bias=b1sb[:, fq * 8 + fh: fq * 8 + fh + 1]),
                              (bps_, bb1), (bh16,))

                        def mm_w2a(E, S_, fh=fh):
                            for oc in range(6):
                                dst = (psS0, psS1, psY)[oc // 2]
                                dcol = (oc % 2) * 512
                                ins = E.matmul(
                                    dst[:, dcol:dcol + 512],
                                    w2q[:, fh * 1024 + oc * 128: fh * 1024 + oc * 128 + 128],
                                    h16[:, fh * 512:(fh + 1) * 512],
                                    start=(fh == 0), stop=(fh == 7))
                            return ins
                        p.add("tensor", mm_w2a, (h.bvc0, h.bvc1, bh16),
                              (h.bpsS0, h.bpsS1, h.bpsY))
                    def mm_w2b(E, S_):
                        for fh in range(8):
                            for oci, oc in enumerate((6, 7)):
                                dst = (psO, psw)[oci]
                                ins = E.matmul(
                                    dst[:, 0:512],
                                    w2q[:, fh * 1024 + oc * 128: fh * 1024 + oc * 128 + 128],
                                    h16[:, fh * 512:(fh + 1) * 512],
                                    start=(fh == 0), stop=(fh == 7))
                        return ins
                    p.add("tensor", mm_w2b, (h.bvc0, h.bvc1, bh16), (h.bpsO, h.bpsw))
                    for oc in range(8):
                        if oc < 6:
                            src = (psS0, psS1, psY)[oc // 2][:, (oc % 2) * 512:(oc % 2) * 512 + 512]
                            bsrc = (h.bpsS0, h.bpsS1, h.bpsY)[oc // 2]
                        elif oc == 6:
                            src = psO[:, 0:512]; bsrc = h.bpsO
                        else:
                            src = psw[:, 0:512]; bsrc = h.bpsw
                        p.add("vector",
                              lambda E, S_, oc=oc, tcols=tcols, src=src, e=e:
                              E.scalar_tensor_tensor(
                                  x[:, oc * TOK + tcols.start: oc * TOK + tcols.stop],
                                  src, combB[:, e:e + 1],
                                  x[:, oc * TOK + tcols.start: oc * TOK + tcols.stop],
                                  op0=ALU.mult, op1=ALU.add),
                              (bsrc, bcombB), (bx,))
        for fc in range(8):
            p.add("tensor",
                  lambda E, S_, fc=fc: E.matmul(psw[0:128, 0:1],
                                                b2sb[:, fc * 128:(fc + 1) * 128],
                                                comb8[:, :], start=True, stop=True),
                  (bb2, bcomb8), (h.bpsw,))
            p.add("vector",
                  lambda E, S_, fc=fc: E.tensor_copy(kvsc[:, 0:1], psw[0:128, 0:1]),
                  (h.bpsw,), (bkvsc,))
            p.add("vector",
                  lambda E, S_, fc=fc: E.tensor_scalar(
                      x[:, fc * TOK:(fc + 1) * TOK], x[:, fc * TOK:(fc + 1) * TOK],
                      kvsc[:, 0:1], None, op0=ALU.add),
                  (bkvsc,), (bx,))

    if "ln1" in _B_EN:
        emit_ln(p, env, x[:, :], bx, xn16[:, :], bxn16)

    if "l1prep" in _B_EN:
        wpp = pool1[:, 0:8192]
        for wdram, odram, bodram, do_rope in ((wkT, o_k1, bo["k1"], True),
                                              (wvT, o_v1, bo["v1"], False),
                                              (wqT, o_q1, bo["q1"], True)):
            emit_dma(p, "gpsimd", wpp.rearrange("p (a c) -> p a c", c=1024),
                     wdram.rearrange("(a p) c -> p a c", p=128), (), (h.bkc0, h.bkc1))
            for tt in range(16):
                trow = slice(tt * 128, (tt + 1) * 128)
                for half in range(2):
                    ps_, bps_ = ((psO, h.bpsO), (psw, h.bpsw))[half]

                    def mm_kv(E, S_, trow=trow, half=half, ps_=ps_):
                        for fc in range(8):
                            ins = E.matmul(
                                ps_[:, 0:512],
                                xn16[:, fc * TOK + trow.start: fc * TOK + trow.stop],
                                wpp[:, fc * 1024 + half * 512: fc * 1024 + half * 512 + 512],
                                start=(fc == 0), stop=(fc == 7))
                        return ins
                    p.add("tensor", mm_kv, (h.bkc0, h.bkc1, bxn16), (bps_,))
                if do_rope:
                    emit_dma(p, "sync", ct0[:, :], cosT[trow, :], (), (bct,))
                    emit_dma(p, "sync", st0_[:, :], sinT[trow, :], (), (bst_,))
                    for half, ps_, bps_ in ((0, psO, h.bpsO), (1, psw, h.bpsw)):
                        pv = ps_[:, 0:512].rearrange("p (a two) -> p two a", two=2)
                        ev = pv[:, 0:1, :]
                        od = pv[:, 1:2, :]
                        k16h = kv16[:, half * 512:(half + 1) * 512].rearrange(
                            "p (a two) -> p two a", two=2)
                        re = k16h[:, 0:1, :]
                        ro = k16h[:, 1:2, :]
                        c3 = ct0[:, half * 256:(half + 1) * 256].rearrange(
                            "p (o a) -> p o a", o=1)
                        s3 = st0_[:, half * 256:(half + 1) * 256].rearrange(
                            "p (o a) -> p o a", o=1)
                        m1 = sq32[:, 0:256].rearrange("p (o a) -> p o a", o=1)
                        m2 = kvsc[:, 0:256].rearrange("p (o a) -> p o a", o=1)
                        p.add("vector", lambda E, S_, ev=ev, c3=c3, m1=m1: E.tensor_mul(m1, ev, c3),
                              (bps_, bct), (bsq32,))
                        p.add("vector", lambda E, S_, od=od, s3=s3, m2=m2: E.tensor_mul(m2, od, s3),
                              (bps_, bst_), (bkvsc,))
                        p.add("vector", lambda E, S_, re=re, m1=m1, m2=m2: E.tensor_sub(re, m1, m2),
                              (bsq32, bkvsc), (bkv16,))
                        p.add("vector", lambda E, S_, ev=ev, s3=s3, m1=m1: E.tensor_mul(m1, ev, s3),
                              (bps_, bst_), (bsq32,))
                        p.add("vector", lambda E, S_, od=od, c3=c3, m2=m2: E.tensor_mul(m2, od, c3),
                              (bps_, bct), (bkvsc,))
                        p.add("vector", lambda E, S_, ro=ro, m1=m1, m2=m2: E.tensor_add(ro, m1, m2),
                              (bsq32, bkvsc), (bkv16,))
                else:
                    p.add("vector", lambda E, S_: E.tensor_copy(kv16[:, 0:512], psO[:, 0:512]),
                          (h.bpsO,), (bkv16,))
                    p.add("vector", lambda E, S_: E.tensor_copy(kv16[:, 512:1024], psw[:, 0:512]),
                          (h.bpsw,), (bkv16,))
                emit_dma(p, "sync", odram[trow, :], kv16[:, :], (bkv16,), (bodram,))

    emit_dma(p, "sync", o_x1.rearrange("(a p) t -> p a t", p=128),
             x[:, :].rearrange("p (a t) -> p a t", t=TOK), (bx,), (bo["x1"],))
    emit_dma(p, "sync", o_xn1.rearrange("(a p) t -> p a t", p=128),
             xn16[:, :].rearrange("p (a t) -> p a t", t=TOK), (bxn16,), (bo["xn1"],))

    with nc.Block() as block:
        p.finalize(block)
    st.close()
    return nc

def build_launch_C1():
    nc = bass.Bass("TRN2", target_bir_lowering=False)
    st = Stack()
    x1T = nc.dram_tensor("x1T", [D, TOK], F32, kind="ExternalInput").ap()
    q1r = nc.dram_tensor("q1r", [TOK, D], BF16, kind="ExternalInput").ap()
    KscrI = nc.dram_tensor("KscrI", [NPAIR, 128, NKT * KT], BF16, kind="ExternalInput").ap()
    VscrI = nc.dram_tensor("VscrI", [NPAIR, 128, NKT * VREC], BF16, kind="ExternalInput").ap()
    mskCi = nc.dram_tensor("mskCi", [128, 8 * 512], BF16, kind="ExternalInput").ap()
    id128i = nc.dram_tensor("id128i", [128, 128], BF16, kind="ExternalInput").ap()
    woT = nc.dram_tensor("woT", [D, D], F32, kind="ExternalInput").ap()
    o_x2a = nc.dram_tensor("o_x2a", [D, TOK], F32, kind="ExternalOutput").ap()
    OTd = nc.dram_tensor("OTd", [NH, 64, TOK], BF16, kind="Internal").ap()

    x = sbuf(st, nc, "x", [128, 8 * TOK], F32)
    OTst0 = sbuf(st, nc, "OTst0", [64, 512], BF16)
    OTst1 = sbuf(st, nc, "OTst1", [64, 512], BF16)
    OTtc = sbuf(st, nc, "OTtc", [64, 16 * 512], BF16)
    pool1 = sbuf(st, nc, "pool1", [128, 16512], BF16)
    P0 = sbuf(st, nc, "P0", [128, 1536], BF16)
    P1 = sbuf(st, nc, "P1", [128, 1536], BF16)
    QT0 = sbuf(st, nc, "QT0", [128, TOK], BF16)
    QT1 = sbuf(st, nc, "QT1", [128, TOK], BF16)
    mskC = sbuf(st, nc, "mskC", [128, 8 * 512], BF16)
    id128 = sbuf(st, nc, "id128", [128, 128], BF16)
    Bcol = sbuf(st, nc, "Bcol", [128, 8], F32)
    nrm = sbuf(st, nc, "nrm", [1, 16], F32)
    tnrm = sbuf(st, nc, "tnrm", [1, 2], F32)
    sq = sbuf(st, nc, "sqW", [128, 512], BF16)
    rs = sbuf(st, nc, "rs", [1, 512], F32)
    rbc = sbuf(st, nc, "rbc", [64, 512], F32)
    ones_col16 = sbuf(st, nc, "ones_col16", [128, 1], BF16)
    ones_row = sbuf(st, nc, "ones_row", [1, 128], F32)
    ones_row64 = sbuf(st, nc, "ones_row64", [1, 64], F32)
    wo64 = sbuf(st, nc, "wo64", [64, 16 * 128], BF16)

    psS0 = psum(st, nc, "psS0", [128, 1024])
    psS1 = psum(st, nc, "psS1", [128, 1024])
    psY = psum(st, nc, "psY", [128, 1024])
    psO = psum(st, nc, "psO", [128, 512])
    psw = psum(st, nc, "psw", [128, 512])

    sems = {e: st.enter(nc.semaphore("sem_" + e)) for e in ENGINES}
    p = Prog(nc, sems)

    h = NS()
    h.kfull = pool1[:, 0:8192]
    h.vfull = pool1[:, 8192:16512]
    h.psS0, h.psS1, h.psO, h.psw = psS0, psS1, psO, psw
    h.psY = psY
    h.P0, h.P1, h.mskC, h.id128, h.Bcol = P0, P1, mskC, id128, Bcol
    h.rs, h.rbc = rs, rbc
    h.OTst0, h.OTst1, h.OTd = OTst0, OTst1, OTd
    h.ones_row64, h.ones_row, h.ones_col16 = ones_row64, ones_row, ones_col16
    h.nrm, h.tnrm, h.sq = nrm, tnrm, sq
    h.psn = psw[0:1, 0:512]
    for nm in ("bkc0", "bkc1", "bvc0", "bvc1", "bpsS0", "bpsS1", "bpsO", "bpsw",
               "bP0", "bP1", "bmskC", "bid", "bBcol", "brs", "brbc",
               "bOTst0", "bOTst1", "bOTd", "bones", "bnrm", "btnrm", "bsq",
               "bpsY"):
        setattr(h, nm, Buf(nm))
    h.bpsn = h.bpsw
    bOTtc = Buf("OTtc")

    bx = Buf("x")
    bQT = [Buf("QT0"), Buf("QT1")]
    QT = [QT0, QT1]
    bKscr = Buf("KscrI"); bVscr = Buf("VscrI")
    bKscr.w = None
    bwo = Buf("wo64")
    bo_x2a = Buf("o_x2a")

    emit_dma(p, "sync", x[:, :].rearrange("p (a t) -> p a t", t=TOK),
             x1T.rearrange("(a p) t -> p a t", p=128), (), (bx,))
    emit_dma(p, "sync", mskC[:, :], mskCi, (), (h.bmskC,))
    emit_dma(p, "sync", id128[:, :], id128i, (), (h.bid,))
    p.add("vector", lambda E, S_: E.memset(ones_col16[:, :], 1.0), (), (h.bones,))
    p.add("vector", lambda E, S_: E.memset(ones_row[:, :], 1.0), (), (h.bones,))
    p.add("vector", lambda E, S_: E.memset(ones_row64[:, :], 1.0), (), (h.bones,))
    p.add("vector", lambda E, S_: E.memset(nrm[:, :], 0.0), (), (h.bnrm,))

    qpctr = [0]

    def prelude_L1(pair):
        qi = qpctr[0] % 2
        qpctr[0] += 1
        qt, bqt = QT[qi], bQT[qi]
        emit_dma(p, "sync", qt[:, :], q1r[:, pair * 128:(pair + 1) * 128],
                 (), (bqt,), transpose=True)
        for c in range(16):
            cols = slice(c * 512, (c + 1) * 512)
            p.add("scalar",
                  lambda E, S_, cols=cols: E.activation(sq[:, 0:512], h.kfull[:, cols], AF.Square),
                  (h.bkc0, h.bkc1), (h.bsq,))
            p.add("tensor",
                  lambda E, S_: E.matmul(h.psn, ones_col16[:, :], sq[:, 0:512],
                                         start=True, stop=True),
                  (h.bsq, h.bones), (h.bpsw,))
            p.add("vector",
                  lambda E, S_: E.tensor_reduce(tnrm[0:1, 0:1], h.psn, axis=AX.X, op=ALU.max),
                  (h.bpsw,), (h.btnrm,))
            p.add("vector",
                  lambda E, S_, pair=pair: E.tensor_max(nrm[0:1, pair:pair + 1],
                                                        nrm[0:1, pair:pair + 1],
                                                        tnrm[0:1, 0:1]),
                  (h.btnrm,), (h.bnrm,))
        emit_pair_norm(p, h, 8 + pair, qt, bqt, TOK)
        emit_bias_from_norms(p, h, pair, nrm[0:1, 8 + pair:9 + pair],
                             nrm[0:1, pair:pair + 1], h.bnrm, h.bnrm)
        return qt, bqt

    emit_attention_pairs(p, h, KscrI, VscrI, bKscr, bVscr, prelude_L1)

    for tc in range(4):
        cols = slice(tc * 512, (tc + 1) * 512)
        emit_dma(p, "sync", OTtc[0:64, :].rearrange("d (hh q) -> d hh q", q=512),
                 OTd[:, :, cols].rearrange("hh d q -> d hh q"),
                 (h.bOTd,), (bOTtc,))
        for oc in range(8):
            emit_dma(p, "gpsimd", wo64[:, :].rearrange("p (hh o) -> p hh o", o=128),
                     woT.rearrange("(hh d) o -> d hh o", d=64)[:, :, oc * 128:(oc + 1) * 128],
                     (), (bwo,))

            def mm_wo(E, S_):
                for hh in range(16):
                    ins = E.matmul(psO[:, 0:512],
                                   wo64[:, hh * 128:(hh + 1) * 128],
                                   OTtc[0:64, hh * 512:(hh + 1) * 512],
                                   start=(hh == 0), stop=(hh == 15))
                return ins
            p.add("tensor", mm_wo, (bwo, bOTtc), (h.bpsO,))
            p.add("vector",
                  lambda E, S_, oc=oc, cols=cols: E.scalar_tensor_tensor(
                      x[:, oc * TOK + cols.start: oc * TOK + cols.stop],
                      psO[:, 0:512], 1.0,
                      x[:, oc * TOK + cols.start: oc * TOK + cols.stop],
                      op0=ALU.mult, op1=ALU.add),
                  (h.bpsO,), (bx,))

    emit_dma(p, "sync", o_x2a.rearrange("(a p) t -> p a t", p=128),
             x[:, :].rearrange("p (a t) -> p a t", t=TOK), (bx,), (bo_x2a,))

    with nc.Block() as block:
        p.finalize(block)
    st.close()
    return nc


def build_launch_C2():
    nc = bass.Bass("TRN2", target_bir_lowering=False)
    st = Stack()
    x2aT = nc.dram_tensor("x2aT", [D, TOK], F32, kind="ExternalInput").ap()
    xn1i = nc.dram_tensor("xn1i", [D, TOK], BF16, kind="ExternalInput").ap()
    w1T = nc.dram_tensor("w1T", [D, FF], F32, kind="ExternalInput").ap()
    w2T = nc.dram_tensor("w2T", [D, FF], F32, kind="ExternalInput").ap()
    h1T = nc.dram_tensor("h1T", [FF, FF], F32, kind="ExternalInput").ap()
    h2T = nc.dram_tensor("h2T", [FF, FF], F32, kind="ExternalInput").ap()
    w3T = nc.dram_tensor("w3T", [FF, D], F32, kind="ExternalInput").ap()
    s1c = nc.dram_tensor("s1c", [128, 32], F32, kind="ExternalInput").ap()
    s2c = nc.dram_tensor("s2c", [128, 32], F32, kind="ExternalInput").ap()
    h1sc = nc.dram_tensor("h1sc", [128, 32], F32, kind="ExternalInput").ap()
    h2sc = nc.dram_tensor("h2sc", [128, 32], F32, kind="ExternalInput").ap()
    w3sc = nc.dram_tensor("w3sc", [128, 8], F32, kind="ExternalInput").ap()
    o_x2 = nc.dram_tensor("o_x2", [D, TOK], F32, kind="ExternalOutput").ap()

    x = sbuf(st, nc, "x", [128, 8 * TOK], F32)
    xn16 = sbuf(st, nc, "xn16", [128, 8 * TOK], BF16)
    pool1 = sbuf(st, nc, "pool1", [128, 16384], BF16)
    fA = sbuf(st, nc, "fA", [128, 16384], BF16)
    fB = sbuf(st, nc, "fB", [128, 16384], BF16)
    sqscr = sbuf(st, nc, "sqscr", [128, 512], F32)
    scl = sbuf(st, nc, "scl", [128, 4 * 32 + 8], F32)

    psS0 = psum(st, nc, "psS0", [128, 1536])
    psS1 = psum(st, nc, "psS1", [128, 1536])
    psO = psum(st, nc, "psO", [128, 512])
    psw = psum(st, nc, "psw", [128, 512])

    sems = {e: st.enter(nc.semaphore("sem_" + e)) for e in ENGINES}
    p = Prog(nc, sems)

    bx = Buf("x"); bxn16 = Buf("xn16")
    bpA = Buf("poolA"); bpB = Buf("poolB")
    bfA = Buf("fA"); bfB = Buf("fB")
    bsq = Buf("sqscr"); bscl = Buf("scl")
    bpsS0 = Buf("psS0"); bpsS1 = Buf("psS1"); bpsO = Buf("psO"); bpsw = Buf("psw")
    bo_x2 = Buf("o_x2")

    emit_dma(p, "sync", x[:, :].rearrange("p (a t) -> p a t", t=TOK),
             x2aT.rearrange("(a p) t -> p a t", p=128), (), (bx,))
    emit_dma(p, "sync", xn16[:, :].rearrange("p (a t) -> p a t", t=TOK),
             xn1i.rearrange("(a p) t -> p a t", p=128), (), (bxn16,))
    emit_dma(p, "sync", scl[:, 0:32], s1c, (), (bscl,))
    emit_dma(p, "sync", scl[:, 32:64], s2c, (), (bscl,))
    emit_dma(p, "sync", scl[:, 64:96], h1sc, (), (bscl,))
    emit_dma(p, "sync", scl[:, 96:128], h2sc, (), (bscl,))
    emit_dma(p, "sync", scl[:, 128:136], w3sc, (), (bscl,))

    for t16 in range(4):
        tcols = slice(t16 * 512, (t16 + 1) * 512)
        for wq in range(4):
            emit_dma(p, "gpsimd", pool1[:, 0:8192].rearrange("p (a c) -> p a c", c=1024),
                     w1T[:, wq * 1024:(wq + 1) * 1024].rearrange("(a p) c -> p a c", p=128),
                     (), (bpA,))
            emit_dma(p, "gpsimd", pool1[:, 8192:16384].rearrange("p (a c) -> p a c", c=1024),
                     w2T[:, wq * 1024:(wq + 1) * 1024].rearrange("(a p) c -> p a c", p=128),
                     (), (bpB,))
            for fh in range(8):
                fhg = wq * 8 + fh

                def mm_y1(E, S_, fh=fh, tcols=tcols):
                    for fc in range(8):
                        ins = E.matmul(
                            psO[:, 0:512],
                            pool1[:, fc * 1024 + fh * 128: fc * 1024 + fh * 128 + 128],
                            xn16[:, fc * TOK + tcols.start: fc * TOK + tcols.stop],
                            start=(fc == 0), stop=(fc == 7))
                    return ins
                p.add("tensor", mm_y1, (bpA, bxn16), (bpsO,))

                def mm_y2(E, S_, fh=fh, tcols=tcols):
                    for fc in range(8):
                        ins = E.matmul(
                            psw[:, 0:512],
                            pool1[:, 8192 + fc * 1024 + fh * 128: 8192 + fc * 1024 + fh * 128 + 128],
                            xn16[:, fc * TOK + tcols.start: fc * TOK + tcols.stop],
                            start=(fc == 0), stop=(fc == 7))
                    return ins
                p.add("tensor", mm_y2, (bpB, bxn16), (bpsw,))
                p.add("scalar",
                      lambda E, S_, fhg=fhg: E.activation(
                          sqscr[:, :], psO[:, 0:512], AF.Silu,
                          scale=scl[:, fhg:fhg + 1]),
                      (bpsO, bscl), (bsq,))
                p.add("vector",
                      lambda E, S_, fhg=fhg: E.scalar_tensor_tensor(
                          fA[:, fhg * 512:(fhg + 1) * 512], psw[:, 0:512],
                          scl[:, 32 + fhg:33 + fhg], sqscr[:, :],
                          op0=ALU.mult, op1=ALU.mult),
                      (bpsw, bscl, bsq), (bfA,))

        for (wT, sbase, bin_, inb, bout_, outb) in (
                (h1T, 64, bfA, fA, bfB, fB), (h2T, 96, bfB, fB, bfA, fA)):
            for ocp in range(16):
                pslab = (bpA, bpB)[ocp % 2]
                slab = pool1[:, 0:8192] if ocp % 2 == 0 else pool1[:, 8192:16384]
                emit_dma(p, "gpsimd", slab.rearrange("p (a c) -> p a c", c=256),
                         wT[:, ocp * 256:(ocp + 1) * 256].rearrange("(a p) c -> p a c", p=128),
                         (), (pslab,))
                for oci in range(2):
                    oc = ocp * 2 + oci
                    ps_, bps_ = ((psO, bpsO), (psw, bpsw))[oci]

                    def mm_g(E, S_, oci=oci, slab=slab, inb=inb, tcols=tcols, ps_=ps_):
                        for ic in range(32):
                            ins = E.matmul(
                                ps_[:, 0:512],
                                slab[:, ic * 256 + oci * 128: ic * 256 + oci * 128 + 128],
                                inb[:, ic * 512: ic * 512 + 512],
                                start=(ic == 0), stop=(ic == 31))
                        return ins
                    p.add("tensor", mm_g, (pslab, bin_), (bps_,))
                    p.add("scalar",
                          lambda E, S_, oc=oc, sbase=sbase, outb=outb, ps_=ps_: E.activation(
                              outb[:, oc * 512:(oc + 1) * 512], ps_[:, 0:512], AF.Gelu,
                              scale=scl[:, sbase + oc: sbase + oc + 1]),
                          (bps_, bscl), (bout_,))

        for halfw in range(2):
            emit_dma(p, "gpsimd", pool1[:, 0:16384].rearrange("p (a c) -> p a c", c=1024),
                     w3T[halfw * 2048:(halfw + 1) * 2048, :].rearrange(
                         "(a p) c -> p a c", p=128),
                     (), (bpA, bpB))

            def mm_w3(E, S_, halfw=halfw, tcols=tcols):
                for oc in range(8):
                    dst, dcol = ((psS0, 0), (psS0, 512), (psS0, 1024),
                                 (psS1, 0), (psS1, 512), (psS1, 1024),
                                 (psO, 0), (psw, 0))[oc]
                    for ic in range(16):
                        icg = halfw * 16 + ic
                        ins = E.matmul(
                            dst[:, dcol:dcol + 512],
                            pool1[:, ic * 1024 + oc * 128: ic * 1024 + oc * 128 + 128],
                            fA[:, icg * 512: icg * 512 + 512],
                            start=(halfw == 0 and ic == 0),
                            stop=(halfw == 1 and ic == 15))
                return ins
            p.add("tensor", mm_w3, (bpA, bpB, bfA),
                  (bpsS0, bpsS1, bpsO, bpsw))
        for oc in range(8):
            if oc < 3:
                src = psS0[:, oc * 512:(oc + 1) * 512]; bsrc = bpsS0
            elif oc < 6:
                src = psS1[:, (oc - 3) * 512:(oc - 2) * 512]; bsrc = bpsS1
            elif oc == 6:
                src = psO[:, 0:512]; bsrc = bpsO
            else:
                src = psw[:, 0:512]; bsrc = bpsw
            p.add("vector",
                  lambda E, S_, oc=oc, tcols=tcols, src=src: E.scalar_tensor_tensor(
                      x[:, oc * TOK + tcols.start: oc * TOK + tcols.stop],
                      src, scl[:, 128 + oc:129 + oc],
                      x[:, oc * TOK + tcols.start: oc * TOK + tcols.stop],
                      op0=ALU.mult, op1=ALU.add),
                  (bsrc, bscl), (bx,))

    emit_dma(p, "sync", o_x2.rearrange("(a p) t -> p a t", p=128),
             x[:, :].rearrange("p (a t) -> p a t", t=TOK), (bx,), (bo_x2,))

    with nc.Block() as block:
        p.finalize(block)
    st.close()
    return nc


_DEBUG = None


def _run(nc, in_maps):
    return run_bass_kernel_spmd(nc, in_maps, core_ids=list(range(NCORE)))


def kernel(**inp):
    x0 = np.asarray(inp["L0_attn_x"], dtype=np.float32)
    f32 = lambda k: np.ascontiguousarray(np.asarray(inp[k], dtype=np.float32))

    id128_np = _bf(np.eye(128, dtype=np.float32))
    mskC_np = build_maskC()
    cos_np, sin_np = zip(*[rope_tables(j) for j in range(4)])

    shards = [shard_fmajor(x0[b]) for b in range(B)]

    ncA = build_launch_A()
    wdkv = f32("L0_attn_W_dkv")
    in_maps = []
    for c in range(NCORE):
        b, j = divmod(c, 4)
        in_maps.append({"xT": shards[b][j], "wdkv": wdkv})
    resA = _run(ncA, in_maps).results
    if _DEBUG is not None:
        _DEBUG["A"] = resA

    ckvF = []
    seqps = []
    for b in range(B):
        ck = np.zeros((S, DC), dtype=ml_dtypes.bfloat16)
        for j in range(4):
            ck[_OWN_IDX[j], :] = resA[4 * b + j]["o_ckv"].T
        ckvF.append(np.ascontiguousarray(ck.T))
        seqps.append(np.stack([resA[4 * b + j]["o_sq"] for j in range(4)]))

    ncB = build_launch_B()
    wq, wk, wv = f32("L1_attn_wq"), f32("L1_attn_wk"), f32("L1_attn_wv")
    b1 = f32("L0_ffn_b1").reshape(NE, 32, 128).transpose(0, 2, 1)
    b1 = np.ascontiguousarray(b1)
    in_maps = []
    for c in range(NCORE):
        b, j = divmod(c, 4)
        in_maps.append({
            "xT": shards[b][j],
            "xn0i": resA[c]["o_xn"],
            "ckvF": ckvF[b],
            "seqps": seqps[b],
            "mskCi": mskC_np[j],
            "id128i": id128_np,
            "w_dq": f32("L0_attn_W_dq"),
            "w_uq": f32("L0_attn_W_uq"),
            "w_uk": f32("L0_attn_W_uk"),
            "w_uv": f32("L0_attn_W_uv"),
            "w_o": f32("L0_attn_W_o"),
            "gate_w": f32("L0_ffn_gate_w"),
            "w1": f32("L0_ffn_w1"),
            "b1T": b1,
            "w2": f32("L0_ffn_w2"),
            "b2": f32("L0_ffn_b2"),
            "wqT": np.ascontiguousarray(wq.T),
            "wkT": np.ascontiguousarray(wk.T),
            "wvT": np.ascontiguousarray(wv.T),
            "cosT": cos_np[j],
            "sinT": sin_np[j],
        })
    resB = _run(ncB, in_maps).results
    if _DEBUG is not None:
        _DEBUG["B"] = resB

    KscrH, VscrH = [], []
    for b in range(B):
        K1 = gather_tokmajor([np.asarray(resB[4 * b + j]["o_k1"]) for j in range(4)])
        V1 = gather_tokmajor([np.asarray(resB[4 * b + j]["o_v1"]) for j in range(4)])
        Ks = np.stack([np.ascontiguousarray(K1[:, 128 * pp:128 * (pp + 1)].T)
                       for pp in range(NPAIR)])
        V4 = V1.reshape(NKT, KT, NH, DK)
        Vs = np.ones((NPAIR, KT, NKT, VREC), dtype=ml_dtypes.bfloat16)
        for pp in range(NPAIR):
            Vs[pp, :, :, 0:64] = V4[:, :, 2 * pp, :].transpose(1, 0, 2)
            Vs[pp, :, :, 65:129] = V4[:, :, 2 * pp + 1, :].transpose(1, 0, 2)
        KscrH.append(Ks)
        VscrH.append(Vs.reshape(NPAIR, KT, NKT * VREC))

    ncC1 = build_launch_C1()
    wo = f32("L1_attn_wo")
    in_maps = []
    for c in range(NCORE):
        b, j = divmod(c, 4)
        in_maps.append({
            "x1T": resB[c]["o_x1"],
            "q1r": resB[c]["o_q1"],
            "KscrI": KscrH[b],
            "VscrI": VscrH[b],
            "mskCi": mskC_np[j],
            "id128i": id128_np,
            "woT": np.ascontiguousarray(wo.T),
        })
    resC1 = _run(ncC1, in_maps).results
    if _DEBUG is not None:
        _DEBUG["C1"] = resC1

    ncC2 = build_launch_C2()
    sc = lambda k: np.ascontiguousarray(f32(k).reshape(-1, 128).T)
    in_maps = []
    for c in range(NCORE):
        b, j = divmod(c, 4)
        in_maps.append({
            "x2aT": resC1[c]["o_x2a"],
            "xn1i": resB[c]["o_xn1"],
            "w1T": np.ascontiguousarray(f32("L1_ffn_w1_w").T),
            "w2T": np.ascontiguousarray(f32("L1_ffn_w2_w").T),
            "h1T": np.ascontiguousarray(f32("L1_ffn_h1_w").T),
            "h2T": np.ascontiguousarray(f32("L1_ffn_h2_w").T),
            "w3T": np.ascontiguousarray(f32("L1_ffn_w3_w").T),
            "s1c": sc("L1_ffn_w1_s"),
            "s2c": sc("L1_ffn_w2_s"),
            "h1sc": sc("L1_ffn_h1_s"),
            "h2sc": sc("L1_ffn_h2_s"),
            "w3sc": sc("L1_ffn_w3_s"),
        })
    resC2 = _run(ncC2, in_maps).results
    if _DEBUG is not None:
        _DEBUG["C2"] = resC2

    out = np.empty((B, S, D), dtype=np.float32)
    for b in range(B):
        out[b] = unshard_fmajor([np.asarray(resC2[4 * b + j]["o_x2"]) for j in range(4)])
    return out
